# version 01
# baseline (speedup 1.0000x reference)
import numpy as np
import ml_dtypes
from contextlib import ExitStack

import concourse.bass as bass
import concourse.mybir as mybir
from concourse.bass_utils import run_bass_kernel_spmd

F32 = mybir.dt.float32
BF16 = mybir.dt.bfloat16
I32 = mybir.dt.int32
Alu = mybir.AluOpType
Act = mybir.ActivationFunctionType
AX = mybir.AxisListType.X

NCORES = 8
C = 2048
TOK = 1024
NT = TOK // 128
KC = C // 128
FB = 12
MLP = 16384
EPS_LN = 1e-5
EPS_ATT = 1e-15

CONSTS = [7.5, -1.0, 0.5, -4.0, -10.0, 0.25, EPS_LN]
A1_DEBUG = set()


def _r(ap, spec, **kw):
    return ap.rearrange(spec, **kw)


def _emit_ln_stats(e, lnt, xc, scr):
    e.tensor_reduce(lnt[:, 0:1], xc, AX, Alu.add)
    e.tensor_tensor(scr, xc, xc, Alu.mult)
    e.drain(fusable=False)
    e.tensor_reduce(lnt[:, 1:2], scr, AX, Alu.add)
    e.tensor_scalar(lnt[:, 2:3], lnt[:, 0:1], 1.0 / C, None, Alu.mult)
    e.drain(fusable=False)
    e.tensor_tensor(lnt[:, 3:4], lnt[:, 2:3], lnt[:, 2:3], Alu.mult)
    e.tensor_scalar(lnt[:, 7:8], lnt[:, 1:2], 1.0 / C, None, Alu.mult)
    e.drain(fusable=False)
    e.tensor_tensor(lnt[:, 4:5], lnt[:, 7:8], lnt[:, 3:4], Alu.subtract)
    return e.drain(fusable=False)


def _emit_rstd(e, lnt):
    e.reciprocal(lnt[:, 5:6], lnt[:, 5:6])
    e.drain(fusable=False)
    e.tensor_tensor(lnt[:, 6:7], lnt[:, 2:3], lnt[:, 5:6], Alu.mult)
    e.drain(fusable=False)
    e.tensor_scalar(lnt[:, 6:7], lnt[:, 6:7], -1.0, None, Alu.mult)
    e.drain(fusable=False)


def build_program(part="full"):
    nc = bass.Bass("TRN2", target_bir_lowering=False)

    x_sl = nc.dram_tensor("x_sl", [TOK, C], F32, kind="ExternalInput").ap()
    wqkv = nc.dram_tensor("wqkv", [3 * C, C], F32, kind="ExternalInput").ap()
    wproj = nc.dram_tensor("wproj", [C, C], F32, kind="ExternalInput").ap()
    bproj = nc.dram_tensor("bproj", [1, C], F32, kind="ExternalInput").ap()
    gwp = nc.dram_tensor("gwp", [MLP, 128], I32, kind="ExternalInput").ap()
    gsc = nc.dram_tensor("gsc", [MLP, 8], F32, kind="ExternalInput").ap()
    uwp = nc.dram_tensor("uwp", [MLP, 128], I32, kind="ExternalInput").ap()
    usc = nc.dram_tensor("usc", [MLP, 8], F32, kind="ExternalInput").ap()
    dwp = nc.dram_tensor("dwp", [C, 1024], I32, kind="ExternalInput").ap()
    dsc = nc.dram_tensor("dsc", [C, 64], F32, kind="ExternalInput").ap()
    cident = nc.dram_tensor("cident", [128, 128], BF16, kind="ExternalInput").ap()
    csel64 = nc.dram_tensor("csel64", [64, KC * 128], BF16, kind="ExternalInput").ap()
    ccf32 = nc.dram_tensor("ccf32", [128, len(CONSTS)], F32, kind="ExternalInput").ap()
    ccbf = nc.dram_tensor("ccbf", [128, len(CONSTS)], BF16, kind="ExternalInput").ap()
    out = nc.dram_tensor("out", [TOK, C], F32, kind="ExternalOutput").ap()

    wqkv_bf = nc.dram_tensor("wqkv_bf", [3 * C, C], BF16)
    wproj_bf = nc.dram_tensor("wproj_bf", [C, C], BF16)
    s_ar_i = nc.dram_tensor("s_ar_i", [128, KC * 132], F32)
    s_ar_o = nc.dram_tensor("s_ar_o", [128, KC * 132], F32)
    dq_g = nc.dram_tensor("dq_g", [128, 32768], BF16)
    dq_u = nc.dram_tensor("dq_u", [128, 32768], BF16)
    dq_d = nc.dram_tensor("dq_d", [128, 32768], BF16)
    ag_g = nc.dram_tensor("ag_g", [NCORES * 128, 32768], BF16)
    ag_u = nc.dram_tensor("ag_u", [NCORES * 128, 32768], BF16)
    ag_d = nc.dram_tensor("ag_d", [NCORES * 128, 32768], BF16)

    with ExitStack() as octx:
        sem = {}
        for s in ["ld", "bp", "wc", "wcp", "x0", "x1", "lnst", "lnrs", "ln1", "t1", "t1e",
                  "wqt0", "wqt1", "qkvm", "qkve", "qt", "qte", "m1", "m1e", "ar", "cc",
                  "sbd", "m2d", "m2de", "R", "m2rb", "m2nm", "m2e",
                  "wpt0", "wpt1", "prm", "pre", "ln2st", "ln2rs", "x2", "xr",
                  "pk0", "pk1", "sc", "dql", "dqsg", "dqw", "dqt", "dqv", "dqtr", "dqte",
                  "stg0", "stg1",
                  "ln3st", "ln3rs", "ln3", "t3", "t3e",
                  "wg0", "wg1", "wu0", "wu1", "wd", "fnb", "fna", "fne", "dnm", "dne",
                  "ln4st", "ln4rs", "outs"]:
            sem[s] = octx.enter_context(nc.semaphore("s_" + s))

        t_ident = octx.enter_context(nc.sbuf_tensor("t_ident", [128, 128], BF16))
        t_sel64 = octx.enter_context(nc.sbuf_tensor("t_sel64", [64, KC * 128], BF16))
        t_cf32 = octx.enter_context(nc.sbuf_tensor("t_cf32", [128, len(CONSTS)], F32))
        t_cbf = octx.enter_context(nc.sbuf_tensor("t_cbf", [128, len(CONSTS)], BF16))
        t_ones1 = octx.enter_context(nc.sbuf_tensor("t_ones1", [1, 128], BF16))
        t_bpb = octx.enter_context(nc.sbuf_tensor("t_bpb", [1, C], BF16))
        for i, v in enumerate(CONSTS):
            nc.const_aps.register(F32, v, t_cf32[:, i:i + 1])
            nc.const_aps.register(BF16, v, t_cbf[:, i:i + 1])

        ps = [octx.enter_context(nc.psum_tensor(f"ps{i}", [128, 512], F32)) for i in range(6)]
        psb = [octx.enter_context(nc.psum_tensor(f"psb{i}", [128, 1024], BF16)) for i in range(2)]

        with ExitStack() as actx:
            x1F = actx.enter_context(nc.sbuf_tensor("x1F", [128, KC * TOK], BF16))
            q_F = x1F
            S_full = actx.enter_context(nc.sbuf_tensor("S_full", [128, KC * 132], F32))

            with ExitStack() as mctx:
                q_T = mctx.enter_context(nc.sbuf_tensor("q_T", [128, NT * C], BF16))
                k_T = mctx.enter_context(nc.sbuf_tensor("k_T", [128, NT * C], BF16))
                v_T = mctx.enter_context(nc.sbuf_tensor("v_T", [128, NT * 2112], BF16))

                with ExitStack() as c1, nc.Block() as blk:
                    xin = [c1.enter_context(nc.sbuf_tensor(f"xin{i}", [128, C], F32)) for i in range(2)]
                    x1ln = [c1.enter_context(nc.sbuf_tensor(f"x1ln{i}", [128, C], BF16)) for i in range(2)]
                    lnt = c1.enter_context(nc.sbuf_tensor("lnt", [128, 8], F32))
                    scrF1 = c1.enter_context(nc.sbuf_tensor("scrF1", [128, C], F32))

                    @blk.sync
                    def _(e):
                        e.dma_start(t_ident[:], cident).then_inc(sem["ld"], 16)
                        e.dma_start(t_sel64[:], csel64).then_inc(sem["ld"], 16)
                        e.dma_start(t_cf32[:], ccf32).then_inc(sem["ld"], 16)
                        e.dma_start(t_cbf[:], ccbf).then_inc(sem["ld"], 16)
                        for tc in range(NT):
                            if tc >= 2:
                                e.wait_ge(sem["ln1"], tc - 1)
                            e.dma_start(xin[tc % 2][:],
                                        x_sl[tc * 128:(tc + 1) * 128, :]).then_inc(sem[f"x{tc % 2}"], 16)

                    @blk.gpsimd
                    def _(e):
                        e.dma_start(t_bpb[:], bproj).then_inc(sem["bp"], 16)
                        if "nocast" not in A1_DEBUG:
                            for i in range(4):
                                e.dma_start(wqkv_bf.ap()[i * 1536:(i + 1) * 1536, :],
                                            wqkv[i * 1536:(i + 1) * 1536, :]).then_inc(sem["wc"], 16)
                            e.dma_start(wproj_bf.ap(), wproj).then_inc(sem["wcp"], 16)

                    def _evac_x1(e, tc):
                        for kc in range(KC):
                            i = tc * KC + kc
                            e.wait_ge(sem["t1"], i + 1)
                            ins = e.tensor_copy(x1F[:, kc * TOK + tc * 128: kc * TOK + (tc + 1) * 128],
                                                psb[i % 2][:, 0:128])
                            if i == NT * KC - 1:
                                e.drain(fusable=False).then_inc(sem["t1e"], 1)
                            else:
                                ins.then_inc(sem["t1e"], 1)

                    @blk.vector
                    def _(e):
                        e.wait_ge(sem["ld"], 64)
                        e.wait_ge(sem["bp"], 16)
                        e.memset(t_ones1[:], 1.0)
                        if "nomemset" not in A1_DEBUG:
                            e.memset(_r(v_T[:, :], "p (a w) -> p a w", w=132)[:, :, 128:132], 1.0)
                        for tc in range(NT):
                            e.wait_ge(sem[f"x{tc % 2}"], 16 * (tc // 2 + 1))
                            xi = xin[tc % 2]
                            if "noln" in A1_DEBUG:
                                e.tensor_copy(x1ln[tc % 2][:], xi[:])
                            else:
                                _emit_ln_stats(e, lnt, xi[:], scrF1[:]).then_inc(sem["lnst"], 1)
                                e.wait_ge(sem["lnrs"], tc + 1)
                                _emit_rstd(e, lnt)
                                e.tensor_scalar(x1ln[tc % 2][:], xi[:], lnt[:, 5:6], lnt[:, 6:7],
                                                Alu.mult, Alu.add)
                                e.drain(fusable=False).then_inc(sem["ln1"], 1)
                            if "notr" not in A1_DEBUG and tc >= 1:
                                _evac_x1(e, tc - 1)
                        if "notr" not in A1_DEBUG:
                            _evac_x1(e, NT - 1)
                        else:
                            e.nop().then_inc(sem["t1e"], 128)

                    @blk.scalar
                    def _(e):
                        if "noln" in A1_DEBUG:
                            return
                        for tc in range(NT):
                            e.wait_ge(sem["lnst"], tc + 1)
                            e.activation(lnt[:, 5:6], lnt[:, 4:5], Act.Sqrt,
                                         bias=EPS_LN, scale=1.0)
                            e.drain(fusable=False).then_inc(sem["lnrs"], 1)

                    @blk.tensor
                    def _(e):
                        if "notr" in A1_DEBUG:
                            return
                        for tc in range(NT):
                            e.wait_ge(sem["ln1"], tc + 1)
                            for kc in range(KC):
                                i = tc * KC + kc
                                if i >= 2:
                                    e.wait_ge(sem["t1e"], i - 1)
                                e.transpose(psb[i % 2][:, 0:128],
                                            x1ln[tc % 2][:, kc * 128:(kc + 1) * 128],
                                            t_ident[:]).then_inc(sem["t1"], 1)

                if part == "a1":
                    dbg1 = nc.dram_tensor("dbg1", [128, KC * TOK], BF16, kind="ExternalOutput").ap()
                    with nc.Block() as blk:
                        @blk.sync
                        def _(e):
                            e.wait_ge(sem["t1e"], 128)
                            e.dma_start(dbg1, x1F[:]).then_inc(sem["outs"], 16)
                            e.wait_ge(sem["outs"], 16)
                    return nc

                with ExitStack() as c2, nc.Block() as blk:
                    wqT = [c2.enter_context(nc.sbuf_tensor(f"wqT{i}", [128, KC * 512], BF16))
                           for i in range(2)]

                    @blk.sync
                    def _(e):
                        e.wait_ge(sem["wc"], 64)
                        for fb in range(FB):
                            if fb >= 2:
                                e.wait_ge(sem["qkvm"], 8 * (fb - 1))
                            for kc in range(KC):
                                e.dma_start_transpose(
                                    wqT[fb % 2][:, kc * 512:(kc + 1) * 512],
                                    wqkv_bf.ap()[fb * 512:(fb + 1) * 512, kc * 128:(kc + 1) * 128],
                                ).then_inc(sem[f"wqt{fb % 2}"], 16)

                    @blk.tensor
                    def _(e):
                        e.wait_ge(sem["t1e"], 128)
                        for fb in range(FB):
                            e.wait_ge(sem[f"wqt{fb % 2}"], 256 * (fb // 2 + 1))
                            for tc in range(NT):
                                g = fb * NT + tc
                                if g >= 4:
                                    e.wait_ge(sem["qkve"], g - 3)
                                pp = ps[g % 4]
                                for kc in range(KC):
                                    ins = e.matmul(pp[:, 0:512],
                                                   x1F[:, kc * TOK + tc * 128: kc * TOK + (tc + 1) * 128],
                                                   wqT[fb % 2][:, kc * 512:(kc + 1) * 512],
                                                   start=(kc == 0), stop=(kc == KC - 1))
                                ins.then_inc(sem["qkvm"], 1)
                        e.wait_ge(sem["qkve"], 32)
                        for tc in range(NT):
                            for cc in range(KC):
                                i = tc * KC + cc
                                if i >= 2:
                                    e.wait_ge(sem["qte"], i - 1)
                                e.transpose(psb[i % 2][:, 0:128],
                                            q_T[:, tc * C + cc * 128: tc * C + (cc + 1) * 128],
                                            t_ident[:]).then_inc(sem["qt"], 1)
                        e.wait_ge(sem["qkve"], 96)
                        for cc in range(KC):
                            if cc >= 2:
                                e.wait_ge(sem["m1e"], cc - 1)
                            pp = ps[4 + cc % 2]
                            for tc in range(NT):
                                ins = e.matmul(pp[:, 0:132],
                                               k_T[:, tc * C + cc * 128: tc * C + (cc + 1) * 128],
                                               v_T[:, tc * 2112 + cc * 132: tc * 2112 + (cc + 1) * 132],
                                               start=(tc == 0), stop=(tc == NT - 1))
                            ins.then_inc(sem["m1"], 1)

                    @blk.vector
                    def _(e):
                        for fb in range(FB):
                            for tc in range(NT):
                                g = fb * NT + tc
                                e.wait_ge(sem["qkvm"], g + 1)
                                pp = ps[g % 4]
                                if fb < 4:
                                    ins = e.tensor_scalar(q_T[:, tc * C + fb * 512: tc * C + (fb + 1) * 512],
                                                          pp[:, 0:512], 0.0, None, Alu.max)
                                elif fb < 8:
                                    ins = e.tensor_scalar(k_T[:, tc * C + (fb - 4) * 512: tc * C + (fb - 3) * 512],
                                                          pp[:, 0:512], 0.0, None, Alu.max)
                                else:
                                    dst = _r(v_T[:, tc * 2112 + (fb - 8) * 528: tc * 2112 + (fb - 7) * 528],
                                             "p (a w) -> p a w", w=132)[:, :, 0:128]
                                    ins = e.tensor_copy(dst, _r(pp[:, 0:512], "p (a w) -> p a w", w=128))
                                if g in (31, 95):
                                    e.drain(fusable=False).then_inc(sem["qkve"], 1)
                                else:
                                    ins.then_inc(sem["qkve"], 1)
                        for tc in range(NT):
                            for cc in range(KC):
                                i = tc * KC + cc
                                e.wait_ge(sem["qt"], i + 1)
                                e.tensor_copy(q_F[:, cc * TOK + tc * 128: cc * TOK + (tc + 1) * 128],
                                              psb[i % 2][:, 0:128]).then_inc(sem["qte"], 1)
                        for cc in range(KC):
                            e.wait_ge(sem["m1"], cc + 1)
                            ins = e.tensor_copy(S_full[:, cc * 132:(cc + 1) * 132], ps[4 + cc % 2][:, 0:132])
                            if cc == KC - 1:
                                e.drain(fusable=False).then_inc(sem["m1e"], 1)
                            else:
                                ins.then_inc(sem["m1e"], 1)

                    @blk.gpsimd
                    def _(e):
                        e.wait_ge(sem["m1e"], 16)
                        e.dma_start(s_ar_i.ap(), S_full[:]).then_inc(sem["ar"], 16)
                        e.wait_ge(sem["ar"], 16)
                        e.collective_compute(
                            "AllReduce", Alu.add,
                            replica_groups=[list(range(NCORES))],
                            ins=[s_ar_i.ap().opt()],
                            outs=[s_ar_o.ap().opt()],
                        ).then_inc(sem["cc"])
                        e.wait_ge(sem["cc"], 1)
                        e.dma_start(S_full[:], s_ar_o.ap()).then_inc(sem["ar"], 16)

                if part == "a2":
                    dbgq = nc.dram_tensor("dbgq", [128, KC * TOK], F32, kind="ExternalOutput").ap()
                    dbgk = nc.dram_tensor("dbgk", [128, NT * C], F32, kind="ExternalOutput").ap()
                    dbgv = nc.dram_tensor("dbgv", [128, NT * 2112], F32, kind="ExternalOutput").ap()
                    dbgs = nc.dram_tensor("dbgs", [128, KC * 132], F32, kind="ExternalOutput").ap()
                    with nc.Block() as blk:
                        @blk.gpsimd
                        def _(e):
                            e.wait_ge(sem["qte"], 128)
                            e.dma_start(dbgq, q_F[:]).then_inc(sem["outs"], 16)
                            e.dma_start(dbgk, k_T[:]).then_inc(sem["outs"], 16)
                            e.dma_start(dbgv, v_T[:]).then_inc(sem["outs"], 16)
                            e.wait_ge(sem["ar"], 32)
                            e.dma_start(dbgs, S_full[:]).then_inc(sem["outs"], 16)
                            e.wait_ge(sem["outs"], 64)
                    return nc

            x2 = octx.enter_context(nc.sbuf_tensor("x2", [128, NT * C], F32, side="right"))

            with ExitStack() as c3, nc.Block() as blk:
                S_bd = c3.enter_context(nc.sbuf_tensor("S_bd", [128, KC * 192], BF16))
                attnF = c3.enter_context(nc.sbuf_tensor("attnF", [128, KC * 512], BF16))
                wpT = c3.enter_context(nc.sbuf_tensor("wpT", [128, KC * 512], BF16))
                apj = c3.enter_context(nc.sbuf_tensor("apj", [128, 4 * C], BF16))
                xin2 = c3.enter_context(nc.sbuf_tensor("xin2", [128, C], F32))
                Dh = c3.enter_context(nc.sbuf_tensor("Dh", [64, 1024], F32))
                Rf = c3.enter_context(nc.sbuf_tensor("Rf", [64, 1024], F32))
                Rb64 = c3.enter_context(nc.sbuf_tensor("Rb64", [64, 1024], BF16))
                Rbs = [c3.enter_context(nc.sbuf_tensor(f"Rbs{i}", [128, 512], F32)) for i in range(2)]
                lnt2 = c3.enter_context(nc.sbuf_tensor("lnt2", [128, 8], F32))
                scrF2 = c3.enter_context(nc.sbuf_tensor("scrF2", [128, C], F32))

                @blk.vector
                def _(e):
                    e.wait_ge(sem["ar"], 32)
                    e.memset(S_bd[:], 0.0)
                    e.drain(fusable=False)
                    for cc in range(KC):
                        for h in range(4):
                            e.tensor_copy(S_bd[32 * h:32 * (h + 1), cc * 192 + 32 * h: cc * 192 + 32 * (h + 1)],
                                          S_full[32 * h:32 * (h + 1), cc * 132 + 32 * h: cc * 132 + 32 * (h + 1)])
                            e.tensor_copy(
                                S_bd[32 * h:32 * (h + 1),
                                     cc * 192 + 128 + 4 * cc + h: cc * 192 + 129 + 4 * cc + h],
                                S_full[32 * h:32 * (h + 1), cc * 132 + 128 + h: cc * 132 + 129 + h])
                    e.drain(fusable=False).then_inc(sem["sbd"], 1)
                    for hh in range(2):
                        hs = slice(hh * 512, (hh + 1) * 512)
                        e.wait_ge(sem["m2d"], hh + 1)
                        e.tensor_copy(Dh[:, hs], ps[2][0:64, 0:512])
                        e.drain(fusable=False).then_inc(sem["m2de"], 1)
                        e.tensor_scalar(Dh[:, hs], Dh[:, hs], EPS_ATT, None, Alu.add)
                        e.drain(fusable=False)
                        e.reciprocal(Rf[:, hs], Dh[:, hs])
                        e.drain(fusable=False)
                        e.tensor_copy(Rb64[:, hs], Rf[:, hs])
                        e.drain(fusable=False).then_inc(sem["R"], 1)
                        for cc in range(KC):
                            j = hh * KC + cc
                            e.wait_ge(sem["m2rb"], j + 1)
                            e.tensor_copy(Rbs[j % 2][:], ps[4 + j % 2][:, 0:512])
                            e.drain(fusable=False)
                            e.wait_ge(sem["m2nm"], j + 1)
                            e.tensor_tensor(attnF[:, cc * 512:(cc + 1) * 512],
                                            ps[j % 2][:, 0:512], Rbs[j % 2][:], Alu.mult)
                            e.drain(fusable=False).then_inc(sem["m2e"], 1)
                        for cb in range(4):
                            for tt in range(4):
                                j = hh * 16 + cb * 4 + tt
                                e.wait_ge(sem["prm"], j + 1)
                                e.tensor_copy(apj[:, tt * C + cb * 512: tt * C + (cb + 1) * 512],
                                              ps[j % 4][:, 0:512]).then_inc(sem["pre"], 1)
                        e.drain(fusable=False)
                        for tt in range(4):
                            tc = hh * 4 + tt
                            ap_c = apj[:, tt * C:(tt + 1) * C]
                            e.wait_ge(sem["xr"], 16 * (tc + 1))
                            _emit_ln_stats(e, lnt2, ap_c, scrF2[:]).then_inc(sem["ln2st"], 1)
                            e.wait_ge(sem["ln2rs"], tc + 1)
                            _emit_rstd(e, lnt2)
                            e.tensor_scalar(xin2[:], xin2[:], lnt2[:, 6:7], None, Alu.add)
                            e.drain(fusable=False)
                            e.scalar_tensor_tensor(x2[:, tc * C:(tc + 1) * C], ap_c, lnt2[:, 5:6],
                                                   xin2[:], Alu.mult, Alu.add)
                            e.drain(fusable=False).then_inc(sem["x2"], 1)

                @blk.scalar
                def _(e):
                    for tc in range(NT):
                        e.wait_ge(sem["ln2st"], tc + 1)
                        e.activation(lnt2[:, 5:6], lnt2[:, 4:5], Act.Sqrt,
                                     bias=EPS_LN, scale=1.0)
                        e.drain(fusable=False).then_inc(sem["ln2rs"], 1)

                @blk.sync
                def _(e):
                    e.wait_ge(sem["wcp"], 16)
                    for hh in range(2):
                        for cb in range(4):
                            jw = hh * 4 + cb
                            if jw >= 1:
                                e.wait_ge(sem["prm"], 4 * jw)
                            for fc in range(KC):
                                e.dma_start_transpose(
                                    wpT[:, fc * 512:(fc + 1) * 512],
                                    wproj_bf.ap()[cb * 512:(cb + 1) * 512, fc * 128:(fc + 1) * 128],
                                ).then_inc(sem["wpt0"], 16)
                        for tt in range(4):
                            tc = hh * 4 + tt
                            if tc >= 1:
                                e.wait_ge(sem["x2"], tc)
                            e.dma_start(xin2[:], x_sl[tc * 128:(tc + 1) * 128, :]).then_inc(sem["xr"], 16)

                @blk.tensor
                def _(e):
                    e.wait_ge(sem["sbd"], 1)
                    for hh in range(2):
                        if hh == 1:
                            e.wait_ge(sem["pre"], 16)
                        if hh == 1:
                            e.wait_ge(sem["m2de"], 1)
                        for cc in range(KC):
                            ins = e.matmul(ps[2][0:64, 0:512],
                                           S_bd[:, cc * 192 + 128: cc * 192 + 192],
                                           q_F[:, cc * TOK + hh * 512: cc * TOK + hh * 512 + 512],
                                           start=(cc == 0), stop=(cc == KC - 1))
                        ins.then_inc(sem["m2d"], 1)
                        e.wait_ge(sem["R"], hh + 1)
                        for cc in range(KC):
                            j = hh * KC + cc
                            if j >= 2:
                                e.wait_ge(sem["m2e"], j - 1)
                            e.matmul(ps[4 + j % 2][:, 0:512],
                                     t_sel64[:, cc * 128:(cc + 1) * 128],
                                     Rb64[:, hh * 512:(hh + 1) * 512],
                                     start=True, stop=True).then_inc(sem["m2rb"], 1)
                            e.matmul(ps[j % 2][:, 0:512],
                                     S_bd[:, cc * 192: cc * 192 + 128],
                                     q_F[:, cc * TOK + hh * 512: cc * TOK + hh * 512 + 512],
                                     start=True, stop=True).then_inc(sem["m2nm"], 1)
                        e.wait_ge(sem["m2e"], hh * KC + KC)
                        for cb in range(4):
                            jw = hh * 4 + cb
                            e.wait_ge(sem["wpt0"], 256 * (jw + 1))
                            for tt in range(4):
                                j = hh * 16 + cb * 4 + tt
                                if j >= 4:
                                    e.wait_ge(sem["pre"], j - 3)
                                pp = ps[j % 4]
                                for fc in range(KC):
                                    e.matmul(pp[:, 0:512],
                                             attnF[:, fc * 512 + tt * 128: fc * 512 + (tt + 1) * 128],
                                             wpT[:, fc * 512:(fc + 1) * 512],
                                             start=(fc == 0), stop=False)
                                e.matmul(pp[:, 0:512], t_ones1[:, :],
                                         t_bpb[:, cb * 512:(cb + 1) * 512],
                                         start=False, stop=True).then_inc(sem["prm"], 1)

        if part == "attn":
            with nc.Block() as blk:
                @blk.sync
                def _(e):
                    for tc in range(NT):
                        e.wait_ge(sem["x2"], tc + 1)
                        e.dma_start(out[tc * 128:(tc + 1) * 128, :],
                                    x2[:, tc * C:(tc + 1) * C]).then_inc(sem["outs"], 16)
                    e.wait_ge(sem["outs"], 16 * NT)
            return nc

        x3F = octx.enter_context(nc.sbuf_tensor("x3F", [128, KC * TOK], BF16, side="right"))

        with ExitStack() as c4, nc.Block() as blk:
            x3ln = [c4.enter_context(nc.sbuf_tensor(f"x3ln{i}", [128, C], BF16)) for i in range(2)]
            lnt3 = c4.enter_context(nc.sbuf_tensor("lnt3", [128, 8], F32))
            scrF3 = c4.enter_context(nc.sbuf_tensor("scrF3", [128, C], F32))
            pbuf = [c4.enter_context(nc.sbuf_tensor(f"pbuf{i}", [128, 1024], I32)) for i in range(2)]
            sc_g = c4.enter_context(nc.sbuf_tensor("sc_g", [128, 1024], F32))
            sc_u = c4.enter_context(nc.sbuf_tensor("sc_u", [128, 1024], F32))
            sc_d = c4.enter_context(nc.sbuf_tensor("sc_d", [128, 1024], F32))
            lo = c4.enter_context(nc.sbuf_tensor("lo", [128, 1024], I32))
            hi = c4.enter_context(nc.sbuf_tensor("hi", [128, 1024], I32))
            cl = c4.enter_context(nc.sbuf_tensor("cl", [128, 1024], BF16))
            ch = c4.enter_context(nc.sbuf_tensor("ch", [128, 1024], BF16))
            dsg = c4.enter_context(nc.sbuf_tensor("dsg", [128, 2048], BF16))
            dw = c4.enter_context(nc.sbuf_tensor("dw", [128, 2048], BF16))
            dt1 = c4.enter_context(nc.sbuf_tensor("dt1", [128, 2048], BF16))
            dt2 = c4.enter_context(nc.sbuf_tensor("dt2", [128, 2048], BF16))
            dm = c4.enter_context(nc.sbuf_tensor("dm", [128, 2048], BF16))
            wout = [c4.enter_context(nc.sbuf_tensor(f"wout{i}", [128, 2048], BF16)) for i in range(2)]
            stg = [c4.enter_context(nc.sbuf_tensor(f"stg{i}", [128, 2048], BF16)) for i in range(2)]

            pieces = ([("g", i) for i in range(16)] + [("u", i) for i in range(16)]
                      + [("d", i) for i in range(16)])

            def stg_tot(b, p):
                t = 0
                for q2 in range(p + 1):
                    if q2 % 2 == b:
                        t += 32 if q2 >= 32 else 16
                return t
            PK = {"g": gwp, "u": uwp, "d": dwp}
            DQ = {"g": dq_g, "u": dq_u, "d": dq_d}
            SCT = {"g": sc_g, "u": sc_u, "d": sc_d}

            @blk.sync
            def _(e):
                e.dma_start(_r(sc_g[:], "p (q nb b) -> p q nb b", q=16, nb=8),
                            _r(gsc, "(q nb p) b -> p q nb b", p=128, nb=8)).then_inc(sem["sc"], 16)
                e.dma_start(_r(sc_u[:], "p (q nb b) -> p q nb b", q=16, nb=8),
                            _r(usc, "(q nb p) b -> p q nb b", p=128, nb=8)).then_inc(sem["sc"], 16)
                e.dma_start(_r(sc_d[:], "p (r b) -> p r b", r=16),
                            _r(dsc, "(r p) b -> p r b", p=128)).then_inc(sem["sc"], 16)
                for pi, (mat, q) in enumerate(pieces):
                    if pi >= 2:
                        e.wait_ge(sem["dql"], pi - 1)
                    if mat in ("g", "u"):
                        e.dma_start(_r(pbuf[pi % 2][:], "p (nb w) -> p nb w", w=128),
                                    _r(PK[mat][q * 1024:(q + 1) * 1024, :], "(nb p) w -> p nb w", p=128)
                                    ).then_inc(sem[f"pk{pi % 2}"], 16)
                    else:
                        e.dma_start(pbuf[pi % 2][:],
                                    PK[mat][q * 128:(q + 1) * 128, :]).then_inc(sem[f"pk{pi % 2}"], 16)

            def _evac_x3(e, tc):
                for kc in range(KC):
                    i = tc * KC + kc
                    e.wait_ge(sem["t3"], i + 1)
                    ins = e.tensor_copy(x3F[:, kc * TOK + tc * 128: kc * TOK + (tc + 1) * 128],
                                        psb[i % 2][:, 0:128])
                    if i == NT * KC - 1:
                        e.drain(fusable=False).then_inc(sem["t3e"], 1)
                    else:
                        ins.then_inc(sem["t3e"], 1)

            @blk.vector
            def _(e):
                for tc in range(NT):
                    e.wait_ge(sem["x2"], tc + 1)
                    xc = x2[:, tc * C:(tc + 1) * C]
                    _emit_ln_stats(e, lnt3, xc, scrF3[:]).then_inc(sem["ln3st"], 1)
                    e.wait_ge(sem["ln3rs"], tc + 1)
                    _emit_rstd(e, lnt3)
                    e.tensor_scalar(x3ln[tc % 2][:], xc, lnt3[:, 5:6], lnt3[:, 6:7], Alu.mult, Alu.add)
                    e.drain(fusable=False).then_inc(sem["ln3"], 1)
                    if tc >= 1:
                        _evac_x3(e, tc - 1)
                _evac_x3(e, NT - 1)
                e.wait_ge(sem["sc"], 48)
                for pi, (mat, q) in enumerate(pieces):
                    e.wait_ge(sem[f"pk{pi % 2}"], 16 * (pi // 2 + 1))
                    pb = pbuf[pi % 2]
                    e.tensor_scalar(lo[:], pb[:], 15, None, Alu.bitwise_and)
                    e.tensor_scalar(hi[:], pb[:], 4, 15, Alu.logical_shift_right, Alu.bitwise_and)
                    e.drain(fusable=False)
                    e.tensor_copy(cl[:], lo[:])
                    e.tensor_copy(ch[:], hi[:])
                    e.drain(fusable=False).then_inc(sem["dql"], 1)
                    e.wait_ge(sem["dqsg"], pi + 1)
                    e.scalar_tensor_tensor(dw[:, 0:1024], dsg[:, 0:1024], 4.0, cl[:], Alu.mult, Alu.add)
                    e.scalar_tensor_tensor(dw[:, 1024:2048], dsg[:, 1024:2048], 4.0, ch[:], Alu.mult, Alu.add)
                    e.drain(fusable=False).then_inc(sem["dqw"], 1)
                    e.wait_ge(sem["dqt"], pi + 1)
                    if pi >= 2:
                        e.wait_ge(sem["dqtr"], 16 * (pi - 1))
                    wo = wout[pi % 2]
                    for half in range(2):
                        s = slice(half * 1024, (half + 1) * 1024)
                        e.scalar_tensor_tensor(dm[:, s], dw[:, s], 0.5, dt1[:, s], Alu.mult, Alu.add)
                        e.drain(fusable=False)
                        e.tensor_tensor(dm[:, s], dm[:, s], dt2[:, s], Alu.add)
                        e.drain(fusable=False)
                        e.scalar_tensor_tensor(dm[:, s], dm[:, s], -2.0, dsg[:, s], Alu.add, Alu.mult)
                        e.drain(fusable=False)
                        wo_h = _r(wo[:], "p (x two) -> p x two", two=2)[:, :, half]
                        if mat in ("g", "u"):
                            sct = _r(_r(SCT[mat][:, q * 64:(q + 1) * 64], "p (nb b) -> p nb b", b=8),
                                     "p nb (b o) -> p nb b o", o=1).broadcast_to([128, 8, 8, 16])
                            dst = _r(wo_h, "p (nb b r) -> p nb b r", nb=8, b=8)
                            src = _r(dm[:, s], "p (nb b r) -> p nb b r", nb=8, b=8)
                        else:
                            sct = _r(SCT[mat][:, q * 64:(q + 1) * 64],
                                     "p (b o) -> p b o", o=1).broadcast_to([128, 64, 16])
                            dst = _r(wo_h, "p (b r) -> p b r", b=64)
                            src = _r(dm[:, s], "p (b r) -> p b r", b=64)
                        e.tensor_tensor(dst, src, sct, Alu.mult)
                    e.drain(fusable=False).then_inc(sem["dqv"], 1)
                    if pi >= 2:
                        e.wait_ge(sem[f"stg{pi % 2}"], stg_tot(pi % 2, pi - 2))
                    for j in range(16):
                        t = pi * 16 + j
                        e.wait_ge(sem["dqtr"], t + 1)
                        ins = e.tensor_copy(stg[pi % 2][:, j * 128:(j + 1) * 128], psb[t % 2][:, 0:128])
                        if j == 15:
                            e.drain(fusable=False).then_inc(sem["dqte"], 1)
                        else:
                            ins.then_inc(sem["dqte"], 1)

            @blk.scalar
            def _(e):
                for tc in range(NT):
                    e.wait_ge(sem["ln3st"], tc + 1)
                    e.activation(lnt3[:, 5:6], lnt3[:, 4:5], Act.Sqrt, bias=EPS_LN, scale=1.0)
                    e.drain(fusable=False).then_inc(sem["ln3rs"], 1)
                for pi in range(len(pieces)):
                    e.wait_ge(sem["dql"], pi + 1)
                    e.activation(dsg[:, 0:1024], cl[:], Act.Sign, bias=7.5, scale=-1.0)
                    e.activation(dsg[:, 1024:2048], ch[:], Act.Sign, bias=7.5, scale=-1.0).then_inc(sem["dqsg"], 1)
                    e.wait_ge(sem["dqw"], pi + 1)
                    e.activation(dt1[:], dw[:], Act.Relu, bias=-4.0, scale=0.5)
                    e.activation(dt2[:], dw[:], Act.Relu, bias=-10.0, scale=1.0).then_inc(sem["dqt"], 1)

            @blk.tensor
            def _(e):
                for tc in range(NT):
                    e.wait_ge(sem["ln3"], tc + 1)
                    for kc in range(KC):
                        i = tc * KC + kc
                        if i >= 2:
                            e.wait_ge(sem["t3e"], i - 1)
                        e.transpose(psb[i % 2][:, 0:128],
                                    x3ln[tc % 2][:, kc * 128:(kc + 1) * 128],
                                    t_ident[:]).then_inc(sem["t3"], 1)
                for pi in range(len(pieces)):
                    e.wait_ge(sem["dqv"], pi + 1)
                    for j in range(16):
                        t = pi * 16 + j
                        if t >= 2:
                            e.wait_ge(sem["dqte"], t - 1)
                        e.transpose(psb[t % 2][:, 0:128],
                                    wout[pi % 2][:, j * 128:(j + 1) * 128],
                                    t_ident[:]).then_inc(sem["dqtr"], 1)

            @blk.gpsimd
            def _(e):
                for pi, (mat, q) in enumerate(pieces):
                    e.wait_ge(sem["dqte"], 16 * (pi + 1))
                    if mat in ("g", "u"):
                        dst = _r(DQ[mat].ap(), "p (nb cbl n) -> p nb cbl n", nb=128, cbl=2)
                        e.dma_start(dst[:, 8 * q:8 * (q + 1), :, :],
                                    _r(stg[pi % 2][:], "p (nb cbl n) -> p nb cbl n", nb=8, cbl=2)
                                    ).then_inc(sem[f"stg{pi % 2}"], 16)
                    else:
                        dst = _r(DQ[mat].ap(), "p (nbl c) -> p nbl c", nbl=16)
                        src = _r(stg[pi % 2][:], "p (nbl c) -> p nbl c", nbl=16)
                        e.dma_start(dst[:, 0:8, q * 128:(q + 1) * 128],
                                    src[:, 0:8, :]).then_inc(sem[f"stg{pi % 2}"], 16)
                        e.dma_start(dst[:, 8:16, q * 128:(q + 1) * 128],
                                    src[:, 8:16, :]).then_inc(sem[f"stg{pi % 2}"], 16)
                e.wait_ge(sem["stg0"], stg_tot(0, 15))
                e.wait_ge(sem["stg1"], stg_tot(1, 15))
                e.collective_compute("AllGather", Alu.bypass, replica_groups=[list(range(NCORES))],
                                     ins=[dq_g.ap().opt()], outs=[ag_g.ap().opt()]).then_inc(sem["cc"])
                e.wait_ge(sem["stg0"], stg_tot(0, 31))
                e.wait_ge(sem["stg1"], stg_tot(1, 31))
                e.collective_compute("AllGather", Alu.bypass, replica_groups=[list(range(NCORES))],
                                     ins=[dq_u.ap().opt()], outs=[ag_u.ap().opt()]).then_inc(sem["cc"])
                e.wait_ge(sem["stg0"], stg_tot(0, 47))
                e.wait_ge(sem["stg1"], stg_tot(1, 47))
                e.collective_compute("AllGather", Alu.bypass, replica_groups=[list(range(NCORES))],
                                     ins=[dq_d.ap().opt()], outs=[ag_d.ap().opt()]).then_inc(sem["cc"])

        with ExitStack() as c5, nc.Block() as blk:
            wg = [c5.enter_context(nc.sbuf_tensor(f"wg{i}", [128, KC * 128], BF16)) for i in range(2)]
            wu = [c5.enter_context(nc.sbuf_tensor(f"wu{i}", [128, KC * 128], BF16)) for i in range(2)]
            wd = c5.enter_context(nc.sbuf_tensor("wd", [128, 8 * C], BF16))
            inter = c5.enter_context(nc.sbuf_tensor("inter", [128, 8 * 512], BF16))
            gsb = [c5.enter_context(nc.sbuf_tensor(f"gsb{i}", [128, 512], BF16)) for i in range(2)]
            dacc = c5.enter_context(nc.sbuf_tensor("dacc", [128, 4 * C], F32))
            lnt4 = c5.enter_context(nc.sbuf_tensor("lnt4", [128, 8], F32))
            scrF4 = c5.enter_context(nc.sbuf_tensor("scrF4", [128, C], F32))

            ag_g5 = _r(ag_g.ap(), "(core p) (nb cbl n) -> p core nb cbl n", p=128, nb=128, cbl=2)
            ag_u5 = _r(ag_u.ap(), "(core p) (nb cbl n) -> p core nb cbl n", p=128, nb=128, cbl=2)

            @blk.sync
            def _(e):
                for th in range(2):
                    for grp in range(16):
                        gg = th * 16 + grp
                        for j in range(8):
                            nb = grp * 8 + j
                            ni = gg * 8 + j
                            if ni >= 2:
                                e.wait_ge(sem["fnb"], ni - 1)
                            if gg == 0 and j == 0:
                                e.wait_ge(sem["cc"], 2)
                            e.dma_start(_r(wg[ni % 2][:], "p (core cbl n) -> p core cbl n",
                                           core=NCORES, cbl=2),
                                        ag_g5[:, :, nb, :, :]).then_inc(sem[f"wg{ni % 2}"], 16)
                            if gg == 0 and j == 0:
                                e.wait_ge(sem["cc"], 3)
                            e.dma_start(_r(wu[ni % 2][:], "p (core cbl n) -> p core cbl n",
                                           core=NCORES, cbl=2),
                                        ag_u5[:, :, nb, :, :]).then_inc(sem[f"wu{ni % 2}"], 16)
                        if gg == 0:
                            e.wait_ge(sem["cc"], 4)
                        else:
                            e.wait_ge(sem["dnm"], 16 * gg)
                        for j in range(8):
                            nb = grp * 8 + j
                            core, nbl = nb // 16, nb % 16
                            e.dma_start(wd[:, j * C:(j + 1) * C],
                                        ag_d.ap()[core * 128:(core + 1) * 128,
                                                  nbl * C:(nbl + 1) * C]).then_inc(sem["wd"], 16)

            @blk.tensor
            def _(e):
                e.wait_ge(sem["pre"], 32)
                e.wait_ge(sem["t3e"], 128)
                for th in range(2):
                    for grp in range(16):
                        gg = th * 16 + grp
                        for j in range(8):
                            ni = gg * 8 + j
                            e.wait_ge(sem[f"wg{ni % 2}"], 16 * (ni // 2 + 1))
                            e.wait_ge(sem[f"wu{ni % 2}"], 16 * (ni // 2 + 1))
                            if ni >= 2:
                                e.wait_ge(sem["fna"], ni - 1)
                                e.wait_ge(sem["fne"], ni - 1)
                            for kc in range(KC):
                                ins = e.matmul(ps[ni % 2][:, 0:512],
                                               wg[ni % 2][:, kc * 128:(kc + 1) * 128],
                                               x3F[:, kc * TOK + th * 512: kc * TOK + th * 512 + 512],
                                               start=(kc == 0), stop=(kc == KC - 1))
                            for kc in range(KC):
                                ins = e.matmul(ps[2 + ni % 2][:, 0:512],
                                               wu[ni % 2][:, kc * 128:(kc + 1) * 128],
                                               x3F[:, kc * TOK + th * 512: kc * TOK + th * 512 + 512],
                                               start=(kc == 0), stop=(kc == KC - 1))
                            ins.then_inc(sem["fnb"], 1)
                        e.wait_ge(sem["wd"], 128 * (gg + 1))
                        e.wait_ge(sem["fne"], gg * 8 + 8)
                        for tq in range(4):
                            for cb in range(4):
                                dj = gg * 16 + tq * 4 + cb
                                if dj >= 2:
                                    e.wait_ge(sem["dne"], dj - 1)
                                for j in range(8):
                                    ins = e.matmul(ps[4 + dj % 2][:, 0:512],
                                                   inter[:, j * 512 + tq * 128: j * 512 + (tq + 1) * 128],
                                                   wd[:, j * C + cb * 512: j * C + (cb + 1) * 512],
                                                   start=(j == 0), stop=(j == 7))
                                ins.then_inc(sem["dnm"], 1)

            @blk.scalar
            def _(e):
                for th in range(2):
                    for grp in range(16):
                        gg = th * 16 + grp
                        for j in range(8):
                            ni = gg * 8 + j
                            e.wait_ge(sem["fnb"], ni + 1)
                            e.activation(gsb[ni % 2][:], ps[ni % 2][:, 0:512], Act.Gelu_apprx_tanh)
                            e.drain(fusable=False).then_inc(sem["fna"], 1)
                    for tq in range(4):
                        e.wait_ge(sem["ln4st"], th * 4 + tq + 1)
                        e.activation(lnt4[:, 5:6], lnt4[:, 4:5], Act.Sqrt, bias=EPS_LN, scale=1.0)
                        e.drain(fusable=False).then_inc(sem["ln4rs"], 1)

            @blk.vector
            def _(e):
                for th in range(2):
                    e.memset(dacc[:], 0.0)
                    e.drain(fusable=False)
                    for grp in range(16):
                        gg = th * 16 + grp
                        for j in range(8):
                            ni = gg * 8 + j
                            e.wait_ge(sem["fna"], ni + 1)
                            ins = e.tensor_tensor(inter[:, j * 512:(j + 1) * 512], gsb[ni % 2][:],
                                                  ps[2 + ni % 2][:, 0:512], Alu.mult)
                            if j == 7:
                                e.drain(fusable=False).then_inc(sem["fne"], 1)
                            else:
                                ins.then_inc(sem["fne"], 1)
                        for tq in range(4):
                            for cb in range(4):
                                dj = gg * 16 + tq * 4 + cb
                                e.wait_ge(sem["dnm"], dj + 1)
                                dslc = dacc[:, tq * C + cb * 512: tq * C + (cb + 1) * 512]
                                e.scalar_tensor_tensor(dslc, ps[4 + dj % 2][:, 0:512], 1.0,
                                                       dslc, Alu.mult, Alu.add).then_inc(sem["dne"], 1)
                    e.drain(fusable=False)
                    for tq in range(4):
                        tc = th * 4 + tq
                        dc = dacc[:, tq * C:(tq + 1) * C]
                        _emit_ln_stats(e, lnt4, dc, scrF4[:]).then_inc(sem["ln4st"], 1)
                        e.wait_ge(sem["ln4rs"], tc + 1)
                        _emit_rstd(e, lnt4)
                        xc = x2[:, tc * C:(tc + 1) * C]
                        e.tensor_scalar(xc, xc, lnt4[:, 6:7], None, Alu.add)
                        e.drain(fusable=False)
                        e.scalar_tensor_tensor(xc, dc, lnt4[:, 5:6], xc, Alu.mult, Alu.add)
                        e.drain(fusable=False).then_inc(sem["x2"], 1)

            @blk.sync
            def _(e):
                for tc in range(NT):
                    e.wait_ge(sem["x2"], NT + tc + 1)
                    e.dma_start(out[tc * 128:(tc + 1) * 128, :],
                                x2[:, tc * C:(tc + 1) * C]).then_inc(sem["outs"], 16)
                e.wait_ge(sem["outs"], 16 * NT)

    return nc


def _make_consts():
    ident = np.eye(128, dtype=ml_dtypes.bfloat16)
    sel64 = np.zeros((64, KC * 128), dtype=ml_dtypes.bfloat16)
    for cc in range(KC):
        for m in range(128):
            sel64[4 * cc + m // 32, cc * 128 + m] = 1.0
    cf32 = np.tile(np.array([CONSTS], dtype=np.float32), (128, 1))
    cbf = cf32.astype(ml_dtypes.bfloat16)
    return ident, sel64, cf32, cbf


def _in_maps(inputs):
    x = np.asarray(inputs["L0_attn_x"])
    wqkv_f = np.asarray(inputs["L0_attn_w_qkv"])
    wproj_f = np.asarray(inputs["L0_attn_w_proj"])
    bproj_f = np.asarray(inputs["L0_attn_b_proj"]).reshape(1, C)
    gwp_f = np.asarray(inputs["L0_ffn_gate_W_packed"])
    gsc_f = np.asarray(inputs["L0_ffn_gate_scales"])
    uwp_f = np.asarray(inputs["L0_ffn_up_W_packed"])
    usc_f = np.asarray(inputs["L0_ffn_up_scales"])
    dwp_f = np.asarray(inputs["L0_ffn_down_W_packed"])
    dsc_f = np.asarray(inputs["L0_ffn_down_scales"])
    ident, sel64, cf32, cbf = _make_consts()
    maps = []
    for i in range(NCORES):
        maps.append(dict(
            x_sl=np.ascontiguousarray(x[0, i * TOK:(i + 1) * TOK, :]),
            wqkv=wqkv_f, wproj=wproj_f, bproj=bproj_f,
            gwp=np.ascontiguousarray(gwp_f[:, i * 128:(i + 1) * 128]),
            gsc=np.ascontiguousarray(gsc_f[:, i * 8:(i + 1) * 8]),
            uwp=np.ascontiguousarray(uwp_f[:, i * 128:(i + 1) * 128]),
            usc=np.ascontiguousarray(usc_f[:, i * 8:(i + 1) * 8]),
            dwp=np.ascontiguousarray(dwp_f[:, i * 1024:(i + 1) * 1024]),
            dsc=np.ascontiguousarray(dsc_f[:, i * 64:(i + 1) * 64]),
            cident=ident, csel64=sel64, ccf32=cf32, ccbf=cbf,
        ))
    return maps


def _run(inputs, part="full"):
    nc = build_program(part=part)
    maps = _in_maps(inputs)
    res = run_bass_kernel_spmd(nc, maps, core_ids=list(range(NCORES)))
    return np.concatenate([r["out"] for r in res.results], axis=0).reshape(1, NCORES * TOK, C)


def kernel(**inputs):
    return _run(inputs, part="full")


# version 02 (final)
# speedup vs baseline: 1.0992x; 1.0992x over previous
import numpy as np
import ml_dtypes
from contextlib import ExitStack

import concourse.bass as bass
import concourse.mybir as mybir
from concourse.bass_utils import run_bass_kernel_spmd

F32 = mybir.dt.float32
BF16 = mybir.dt.bfloat16
I32 = mybir.dt.int32
Alu = mybir.AluOpType
Act = mybir.ActivationFunctionType
AX = mybir.AxisListType.X

NCORES = 8
C = 2048
TOK = 1024
NT = TOK // 128
KC = C // 128
FB = 12
MLP = 16384
EPS_LN = 1e-5
EPS_ATT = 1e-15

CONSTS = [7.5, -1.0, 0.5, -4.0, -10.0, 0.25, EPS_LN]
A1_DEBUG = set()


def _r(ap, spec, **kw):
    return ap.rearrange(spec, **kw)


def _emit_ln_stats(e, lnt, xc, scr):
    e.tensor_reduce(lnt[:, 0:1], xc, AX, Alu.add)
    e.tensor_tensor(scr, xc, xc, Alu.mult)
    e.drain(fusable=False)
    e.tensor_reduce(lnt[:, 1:2], scr, AX, Alu.add)
    e.tensor_scalar(lnt[:, 2:3], lnt[:, 0:1], 1.0 / C, None, Alu.mult)
    e.drain(fusable=False)
    e.tensor_tensor(lnt[:, 3:4], lnt[:, 2:3], lnt[:, 2:3], Alu.mult)
    e.tensor_scalar(lnt[:, 7:8], lnt[:, 1:2], 1.0 / C, None, Alu.mult)
    e.drain(fusable=False)
    e.tensor_tensor(lnt[:, 4:5], lnt[:, 7:8], lnt[:, 3:4], Alu.subtract)
    return e.drain(fusable=False)


def _emit_rstd(e, lnt):
    e.reciprocal(lnt[:, 5:6], lnt[:, 5:6])
    e.drain(fusable=False)
    e.tensor_tensor(lnt[:, 6:7], lnt[:, 2:3], lnt[:, 5:6], Alu.mult)
    e.drain(fusable=False)
    e.tensor_scalar(lnt[:, 6:7], lnt[:, 6:7], -1.0, None, Alu.mult)
    e.drain(fusable=False)


def build_program(part="full"):
    nc = bass.Bass("TRN2", target_bir_lowering=False)

    x_sl = nc.dram_tensor("x_sl", [TOK, C], F32, kind="ExternalInput").ap()
    wqkv = nc.dram_tensor("wqkv", [3 * C, C], F32, kind="ExternalInput").ap()
    wproj = nc.dram_tensor("wproj", [C, C], F32, kind="ExternalInput").ap()
    bproj = nc.dram_tensor("bproj", [1, C], F32, kind="ExternalInput").ap()
    gwp = nc.dram_tensor("gwp", [MLP, 128], I32, kind="ExternalInput").ap()
    gsc = nc.dram_tensor("gsc", [MLP, 8], F32, kind="ExternalInput").ap()
    uwp = nc.dram_tensor("uwp", [MLP, 128], I32, kind="ExternalInput").ap()
    usc = nc.dram_tensor("usc", [MLP, 8], F32, kind="ExternalInput").ap()
    dwp = nc.dram_tensor("dwp", [C, 1024], I32, kind="ExternalInput").ap()
    dsc = nc.dram_tensor("dsc", [C, 64], F32, kind="ExternalInput").ap()
    cident = nc.dram_tensor("cident", [128, 128], BF16, kind="ExternalInput").ap()
    csel64 = nc.dram_tensor("csel64", [64, KC * 128], BF16, kind="ExternalInput").ap()
    ccf32 = nc.dram_tensor("ccf32", [128, len(CONSTS)], F32, kind="ExternalInput").ap()
    ccbf = nc.dram_tensor("ccbf", [128, len(CONSTS)], BF16, kind="ExternalInput").ap()
    out = nc.dram_tensor("out", [TOK, C], F32, kind="ExternalOutput").ap()

    wqkv_bf = nc.dram_tensor("wqkv_bf", [3 * C, C], BF16)
    wproj_bf = nc.dram_tensor("wproj_bf", [C, C], BF16)
    s_ar_i = nc.dram_tensor("s_ar_i", [128, KC * 132], F32)
    s_ar_o = nc.dram_tensor("s_ar_o", [128, KC * 132], F32)
    dq_g = nc.dram_tensor("dq_g", [128, 32768], BF16)
    dq_u = nc.dram_tensor("dq_u", [128, 32768], BF16)
    dq_d = nc.dram_tensor("dq_d", [128, 32768], BF16)
    ag_g = nc.dram_tensor("ag_g", [NCORES * 128, 32768], BF16)
    ag_u = nc.dram_tensor("ag_u", [NCORES * 128, 32768], BF16)
    ag_d = nc.dram_tensor("ag_d", [NCORES * 128, 32768], BF16)

    with ExitStack() as octx:
        sem = {}
        for s in ["ld", "bp", "wc", "wcp", "x0", "x1", "lnst", "lnrs", "ln1", "t1", "t1e",
                  "wqt0", "wqt1", "qkvm", "qkve", "qt", "qte", "m1", "m1e", "ar", "cc",
                  "sbd", "m2d", "m2de", "R", "m2rb", "m2nm", "m2e",
                  "wpt0", "wpt1", "prm", "pre", "ln2st", "ln2rs", "x2", "xr",
                  "pk0", "pk1", "sc", "dql", "dqsg", "dqw", "dqt", "dqv", "dqtr", "dqte",
                  "stg0", "stg1",
                  "ln3st", "ln3rs", "ln3", "t3", "t3e",
                  "wg0", "wg1", "wu0", "wu1", "wd", "fnb", "fna", "fne", "dnm", "dne",
                  "ln4st", "ln4rs", "outs"]:
            sem[s] = octx.enter_context(nc.semaphore("s_" + s))

        t_ident = octx.enter_context(nc.sbuf_tensor("t_ident", [128, 128], BF16))
        t_sel64 = octx.enter_context(nc.sbuf_tensor("t_sel64", [64, KC * 128], BF16))
        t_cf32 = octx.enter_context(nc.sbuf_tensor("t_cf32", [128, len(CONSTS)], F32))
        t_cbf = octx.enter_context(nc.sbuf_tensor("t_cbf", [128, len(CONSTS)], BF16))
        t_ones1 = octx.enter_context(nc.sbuf_tensor("t_ones1", [1, 128], BF16))
        t_bpb = octx.enter_context(nc.sbuf_tensor("t_bpb", [1, C], BF16))
        for i, v in enumerate(CONSTS):
            nc.const_aps.register(F32, v, t_cf32[:, i:i + 1])
            nc.const_aps.register(BF16, v, t_cbf[:, i:i + 1])

        ps = [octx.enter_context(nc.psum_tensor(f"ps{i}", [128, 512], F32)) for i in range(6)]
        psb = [octx.enter_context(nc.psum_tensor(f"psb{i}", [128, 1024], BF16)) for i in range(2)]

        with ExitStack() as actx:
            x1F = actx.enter_context(nc.sbuf_tensor("x1F", [128, KC * TOK], BF16))
            q_F = x1F
            S_full = actx.enter_context(nc.sbuf_tensor("S_full", [128, KC * 132], F32))

            with ExitStack() as mctx:
                q_T = mctx.enter_context(nc.sbuf_tensor("q_T", [128, NT * C], BF16))
                k_T = mctx.enter_context(nc.sbuf_tensor("k_T", [128, NT * C], BF16))
                v_T = mctx.enter_context(nc.sbuf_tensor("v_T", [128, NT * 2112], BF16))

                with ExitStack() as c1, nc.Block() as blk:
                    xin = [c1.enter_context(nc.sbuf_tensor(f"xin{i}", [128, C], F32)) for i in range(2)]
                    x1ln = [c1.enter_context(nc.sbuf_tensor(f"x1ln{i}", [128, C], BF16)) for i in range(2)]
                    lnt = c1.enter_context(nc.sbuf_tensor("lnt", [128, 8], F32))
                    scrF1 = c1.enter_context(nc.sbuf_tensor("scrF1", [128, C], F32))

                    @blk.sync
                    def _(e):
                        e.dma_start(t_ident[:], cident).then_inc(sem["ld"], 16)
                        e.dma_start(t_sel64[:], csel64).then_inc(sem["ld"], 16)
                        e.dma_start(t_cf32[:], ccf32).then_inc(sem["ld"], 16)
                        e.dma_start(t_cbf[:], ccbf).then_inc(sem["ld"], 16)
                        for tc in range(NT):
                            if tc >= 2:
                                e.wait_ge(sem["ln1"], tc - 1)
                            e.dma_start(xin[tc % 2][:],
                                        x_sl[tc * 128:(tc + 1) * 128, :]).then_inc(sem[f"x{tc % 2}"], 16)

                    @blk.gpsimd
                    def _(e):
                        e.dma_start(t_bpb[:], bproj).then_inc(sem["bp"], 16)
                        if "nocast" not in A1_DEBUG:
                            for i in range(4):
                                e.dma_start(wqkv_bf.ap()[i * 1536:(i + 1) * 1536, :],
                                            wqkv[i * 1536:(i + 1) * 1536, :]).then_inc(sem["wc"], 16)
                            e.dma_start(wproj_bf.ap(), wproj).then_inc(sem["wcp"], 16)

                    def _evac_x1(e, tc):
                        for kc in range(KC):
                            i = tc * KC + kc
                            e.wait_ge(sem["t1"], i + 1)
                            ins = e.tensor_copy(x1F[:, kc * TOK + tc * 128: kc * TOK + (tc + 1) * 128],
                                                psb[i % 2][:, 0:128])
                            if i == NT * KC - 1:
                                e.drain(fusable=False).then_inc(sem["t1e"], 1)
                            else:
                                ins.then_inc(sem["t1e"], 1)

                    @blk.vector
                    def _(e):
                        e.wait_ge(sem["ld"], 64)
                        e.wait_ge(sem["bp"], 16)
                        e.memset(t_ones1[:], 1.0)
                        if "nomemset" not in A1_DEBUG:
                            e.memset(_r(v_T[:, :], "p (a w) -> p a w", w=132)[:, :, 128:132], 1.0)
                        for tc in range(NT):
                            e.wait_ge(sem[f"x{tc % 2}"], 16 * (tc // 2 + 1))
                            xi = xin[tc % 2]
                            if "noln" in A1_DEBUG:
                                e.tensor_copy(x1ln[tc % 2][:], xi[:])
                            else:
                                _emit_ln_stats(e, lnt, xi[:], scrF1[:]).then_inc(sem["lnst"], 1)
                                e.wait_ge(sem["lnrs"], tc + 1)
                                _emit_rstd(e, lnt)
                                e.tensor_scalar(x1ln[tc % 2][:], xi[:], lnt[:, 5:6], lnt[:, 6:7],
                                                Alu.mult, Alu.add)
                                e.drain(fusable=False).then_inc(sem["ln1"], 1)
                            if "notr" not in A1_DEBUG and tc >= 1:
                                _evac_x1(e, tc - 1)
                        if "notr" not in A1_DEBUG:
                            _evac_x1(e, NT - 1)
                        else:
                            e.nop().then_inc(sem["t1e"], 128)

                    @blk.scalar
                    def _(e):
                        if "noln" in A1_DEBUG:
                            return
                        for tc in range(NT):
                            e.wait_ge(sem["lnst"], tc + 1)
                            e.activation(lnt[:, 5:6], lnt[:, 4:5], Act.Sqrt,
                                         bias=EPS_LN, scale=1.0)
                            e.drain(fusable=False).then_inc(sem["lnrs"], 1)

                    @blk.tensor
                    def _(e):
                        if "notr" in A1_DEBUG:
                            return
                        for tc in range(NT):
                            e.wait_ge(sem["ln1"], tc + 1)
                            for kc in range(KC):
                                i = tc * KC + kc
                                if i >= 2:
                                    e.wait_ge(sem["t1e"], i - 1)
                                e.transpose(psb[i % 2][:, 0:128],
                                            x1ln[tc % 2][:, kc * 128:(kc + 1) * 128],
                                            t_ident[:]).then_inc(sem["t1"], 1)

                if part == "a1":
                    dbg1 = nc.dram_tensor("dbg1", [128, KC * TOK], BF16, kind="ExternalOutput").ap()
                    with nc.Block() as blk:
                        @blk.sync
                        def _(e):
                            e.wait_ge(sem["t1e"], 128)
                            e.dma_start(dbg1, x1F[:]).then_inc(sem["outs"], 16)
                            e.wait_ge(sem["outs"], 16)
                    return nc

                with ExitStack() as c2, nc.Block() as blk:
                    wqT = [c2.enter_context(nc.sbuf_tensor(f"wqT{i}", [128, KC * 512], BF16))
                           for i in range(2)]

                    @blk.sync
                    def _(e):
                        e.wait_ge(sem["wc"], 64)
                        for fb in range(FB):
                            if fb >= 2:
                                e.wait_ge(sem["qkvm"], 8 * (fb - 1))
                            for kc in range(KC):
                                e.dma_start_transpose(
                                    wqT[fb % 2][:, kc * 512:(kc + 1) * 512],
                                    wqkv_bf.ap()[fb * 512:(fb + 1) * 512, kc * 128:(kc + 1) * 128],
                                ).then_inc(sem[f"wqt{fb % 2}"], 16)

                    @blk.tensor
                    def _(e):
                        e.wait_ge(sem["t1e"], 128)
                        for fb in range(FB):
                            e.wait_ge(sem[f"wqt{fb % 2}"], 256 * (fb // 2 + 1))
                            for tc in range(NT):
                                g = fb * NT + tc
                                if g >= 4:
                                    e.wait_ge(sem["qkve"], g - 3)
                                pp = ps[g % 4]
                                for kc in range(KC):
                                    ins = e.matmul(pp[:, 0:512],
                                                   x1F[:, kc * TOK + tc * 128: kc * TOK + (tc + 1) * 128],
                                                   wqT[fb % 2][:, kc * 512:(kc + 1) * 512],
                                                   start=(kc == 0), stop=(kc == KC - 1))
                                ins.then_inc(sem["qkvm"], 1)
                        e.wait_ge(sem["qkve"], 32)
                        for tc in range(NT):
                            for cc in range(KC):
                                i = tc * KC + cc
                                if i >= 2:
                                    e.wait_ge(sem["qte"], i - 1)
                                e.transpose(psb[i % 2][:, 0:128],
                                            q_T[:, tc * C + cc * 128: tc * C + (cc + 1) * 128],
                                            t_ident[:]).then_inc(sem["qt"], 1)
                        e.wait_ge(sem["qkve"], 96)
                        for cc in range(KC):
                            if cc >= 2:
                                e.wait_ge(sem["m1e"], cc - 1)
                            pp = ps[4 + cc % 2]
                            for tc in range(NT):
                                ins = e.matmul(pp[:, 0:132],
                                               k_T[:, tc * C + cc * 128: tc * C + (cc + 1) * 128],
                                               v_T[:, tc * 2112 + cc * 132: tc * 2112 + (cc + 1) * 132],
                                               start=(tc == 0), stop=(tc == NT - 1))
                            ins.then_inc(sem["m1"], 1)

                    @blk.vector
                    def _(e):
                        for fb in range(FB):
                            for tc in range(NT):
                                g = fb * NT + tc
                                e.wait_ge(sem["qkvm"], g + 1)
                                pp = ps[g % 4]
                                if fb < 4:
                                    ins = e.tensor_scalar(q_T[:, tc * C + fb * 512: tc * C + (fb + 1) * 512],
                                                          pp[:, 0:512], 0.0, None, Alu.max)
                                elif fb < 8:
                                    ins = e.tensor_scalar(k_T[:, tc * C + (fb - 4) * 512: tc * C + (fb - 3) * 512],
                                                          pp[:, 0:512], 0.0, None, Alu.max)
                                else:
                                    dst = _r(v_T[:, tc * 2112 + (fb - 8) * 528: tc * 2112 + (fb - 7) * 528],
                                             "p (a w) -> p a w", w=132)[:, :, 0:128]
                                    ins = e.tensor_copy(dst, _r(pp[:, 0:512], "p (a w) -> p a w", w=128))
                                if g in (31, 95):
                                    e.drain(fusable=False).then_inc(sem["qkve"], 1)
                                else:
                                    ins.then_inc(sem["qkve"], 1)
                        for tc in range(NT):
                            for cc in range(KC):
                                i = tc * KC + cc
                                e.wait_ge(sem["qt"], i + 1)
                                e.tensor_copy(q_F[:, cc * TOK + tc * 128: cc * TOK + (tc + 1) * 128],
                                              psb[i % 2][:, 0:128]).then_inc(sem["qte"], 1)
                        for cc in range(KC):
                            e.wait_ge(sem["m1"], cc + 1)
                            ins = e.tensor_copy(S_full[:, cc * 132:(cc + 1) * 132], ps[4 + cc % 2][:, 0:132])
                            if cc == KC - 1:
                                e.drain(fusable=False).then_inc(sem["m1e"], 1)
                            else:
                                ins.then_inc(sem["m1e"], 1)

                    @blk.gpsimd
                    def _(e):
                        e.wait_ge(sem["m1e"], 16)
                        e.dma_start(s_ar_i.ap(), S_full[:]).then_inc(sem["ar"], 16)
                        e.wait_ge(sem["ar"], 16)
                        e.collective_compute(
                            "AllReduce", Alu.add,
                            replica_groups=[list(range(NCORES))],
                            ins=[s_ar_i.ap().opt()],
                            outs=[s_ar_o.ap().opt()],
                        ).then_inc(sem["cc"])
                        e.wait_ge(sem["cc"], 1)
                        e.dma_start(S_full[:], s_ar_o.ap()).then_inc(sem["ar"], 16)

                if part == "a2":
                    dbgq = nc.dram_tensor("dbgq", [128, KC * TOK], F32, kind="ExternalOutput").ap()
                    dbgk = nc.dram_tensor("dbgk", [128, NT * C], F32, kind="ExternalOutput").ap()
                    dbgv = nc.dram_tensor("dbgv", [128, NT * 2112], F32, kind="ExternalOutput").ap()
                    dbgs = nc.dram_tensor("dbgs", [128, KC * 132], F32, kind="ExternalOutput").ap()
                    with nc.Block() as blk:
                        @blk.gpsimd
                        def _(e):
                            e.wait_ge(sem["qte"], 128)
                            e.dma_start(dbgq, q_F[:]).then_inc(sem["outs"], 16)
                            e.dma_start(dbgk, k_T[:]).then_inc(sem["outs"], 16)
                            e.dma_start(dbgv, v_T[:]).then_inc(sem["outs"], 16)
                            e.wait_ge(sem["ar"], 32)
                            e.dma_start(dbgs, S_full[:]).then_inc(sem["outs"], 16)
                            e.wait_ge(sem["outs"], 64)
                    return nc

            x2 = octx.enter_context(nc.sbuf_tensor("x2", [128, NT * C], F32, side="right"))

            with ExitStack() as c3, nc.Block() as blk:
                S_bd = c3.enter_context(nc.sbuf_tensor("S_bd", [128, KC * 192], BF16))
                attnF = c3.enter_context(nc.sbuf_tensor("attnF", [128, KC * 512], BF16))
                wpT = c3.enter_context(nc.sbuf_tensor("wpT", [128, KC * 512], BF16))
                apj = c3.enter_context(nc.sbuf_tensor("apj", [128, 4 * C], BF16))
                xin2 = c3.enter_context(nc.sbuf_tensor("xin2", [128, C], F32))
                Dh = c3.enter_context(nc.sbuf_tensor("Dh", [64, 1024], F32))
                Rf = c3.enter_context(nc.sbuf_tensor("Rf", [64, 1024], F32))
                Rb64 = c3.enter_context(nc.sbuf_tensor("Rb64", [64, 1024], BF16))
                Rbs = [c3.enter_context(nc.sbuf_tensor(f"Rbs{i}", [128, 512], F32)) for i in range(2)]
                lnt2 = c3.enter_context(nc.sbuf_tensor("lnt2", [128, 8], F32))
                scrF2 = c3.enter_context(nc.sbuf_tensor("scrF2", [128, C], F32))

                @blk.vector
                def _(e):
                    e.wait_ge(sem["ar"], 32)
                    e.memset(S_bd[:], 0.0)
                    e.drain(fusable=False)
                    for cc in range(KC):
                        for h in range(4):
                            e.tensor_copy(S_bd[32 * h:32 * (h + 1), cc * 192 + 32 * h: cc * 192 + 32 * (h + 1)],
                                          S_full[32 * h:32 * (h + 1), cc * 132 + 32 * h: cc * 132 + 32 * (h + 1)])
                            e.tensor_copy(
                                S_bd[32 * h:32 * (h + 1),
                                     cc * 192 + 128 + 4 * cc + h: cc * 192 + 129 + 4 * cc + h],
                                S_full[32 * h:32 * (h + 1), cc * 132 + 128 + h: cc * 132 + 129 + h])
                    e.drain(fusable=False).then_inc(sem["sbd"], 1)
                    for hh in range(2):
                        hs = slice(hh * 512, (hh + 1) * 512)
                        e.wait_ge(sem["m2d"], hh + 1)
                        e.tensor_copy(Dh[:, hs], ps[2][0:64, 0:512])
                        e.drain(fusable=False).then_inc(sem["m2de"], 1)
                        e.tensor_scalar(Dh[:, hs], Dh[:, hs], EPS_ATT, None, Alu.add)
                        e.drain(fusable=False)
                        e.reciprocal(Rf[:, hs], Dh[:, hs])
                        e.drain(fusable=False)
                        e.tensor_copy(Rb64[:, hs], Rf[:, hs])
                        e.drain(fusable=False).then_inc(sem["R"], 1)
                        for cc in range(KC):
                            j = hh * KC + cc
                            e.wait_ge(sem["m2rb"], j + 1)
                            e.tensor_copy(Rbs[j % 2][:], ps[4 + j % 2][:, 0:512])
                            e.drain(fusable=False)
                            e.wait_ge(sem["m2nm"], j + 1)
                            e.tensor_tensor(attnF[:, cc * 512:(cc + 1) * 512],
                                            ps[j % 2][:, 0:512], Rbs[j % 2][:], Alu.mult)
                            e.drain(fusable=False).then_inc(sem["m2e"], 1)
                        for cb in range(4):
                            for tt in range(4):
                                j = hh * 16 + cb * 4 + tt
                                e.wait_ge(sem["prm"], j + 1)
                                e.tensor_copy(apj[:, tt * C + cb * 512: tt * C + (cb + 1) * 512],
                                              ps[j % 4][:, 0:512]).then_inc(sem["pre"], 1)
                        e.drain(fusable=False)
                        for tt in range(4):
                            tc = hh * 4 + tt
                            ap_c = apj[:, tt * C:(tt + 1) * C]
                            e.wait_ge(sem["xr"], 16 * (tc + 1))
                            _emit_ln_stats(e, lnt2, ap_c, scrF2[:]).then_inc(sem["ln2st"], 1)
                            e.wait_ge(sem["ln2rs"], tc + 1)
                            _emit_rstd(e, lnt2)
                            e.tensor_scalar(xin2[:], xin2[:], lnt2[:, 6:7], None, Alu.add)
                            e.drain(fusable=False)
                            e.scalar_tensor_tensor(x2[:, tc * C:(tc + 1) * C], ap_c, lnt2[:, 5:6],
                                                   xin2[:], Alu.mult, Alu.add)
                            e.drain(fusable=False).then_inc(sem["x2"], 1)

                @blk.scalar
                def _(e):
                    for tc in range(NT):
                        e.wait_ge(sem["ln2st"], tc + 1)
                        e.activation(lnt2[:, 5:6], lnt2[:, 4:5], Act.Sqrt,
                                     bias=EPS_LN, scale=1.0)
                        e.drain(fusable=False).then_inc(sem["ln2rs"], 1)

                @blk.sync
                def _(e):
                    e.wait_ge(sem["wcp"], 16)
                    for hh in range(2):
                        for cb in range(4):
                            jw = hh * 4 + cb
                            if jw >= 1:
                                e.wait_ge(sem["prm"], 4 * jw)
                            for fc in range(KC):
                                e.dma_start_transpose(
                                    wpT[:, fc * 512:(fc + 1) * 512],
                                    wproj_bf.ap()[cb * 512:(cb + 1) * 512, fc * 128:(fc + 1) * 128],
                                ).then_inc(sem["wpt0"], 16)
                        for tt in range(4):
                            tc = hh * 4 + tt
                            if tc >= 1:
                                e.wait_ge(sem["x2"], tc)
                            e.dma_start(xin2[:], x_sl[tc * 128:(tc + 1) * 128, :]).then_inc(sem["xr"], 16)

                @blk.tensor
                def _(e):
                    e.wait_ge(sem["sbd"], 1)
                    for hh in range(2):
                        if hh == 1:
                            e.wait_ge(sem["pre"], 16)
                        if hh == 1:
                            e.wait_ge(sem["m2de"], 1)
                        for cc in range(KC):
                            ins = e.matmul(ps[2][0:64, 0:512],
                                           S_bd[:, cc * 192 + 128: cc * 192 + 192],
                                           q_F[:, cc * TOK + hh * 512: cc * TOK + hh * 512 + 512],
                                           start=(cc == 0), stop=(cc == KC - 1))
                        ins.then_inc(sem["m2d"], 1)
                        e.wait_ge(sem["R"], hh + 1)
                        for cc in range(KC):
                            j = hh * KC + cc
                            if j >= 2:
                                e.wait_ge(sem["m2e"], j - 1)
                            e.matmul(ps[4 + j % 2][:, 0:512],
                                     t_sel64[:, cc * 128:(cc + 1) * 128],
                                     Rb64[:, hh * 512:(hh + 1) * 512],
                                     start=True, stop=True).then_inc(sem["m2rb"], 1)
                            e.matmul(ps[j % 2][:, 0:512],
                                     S_bd[:, cc * 192: cc * 192 + 128],
                                     q_F[:, cc * TOK + hh * 512: cc * TOK + hh * 512 + 512],
                                     start=True, stop=True).then_inc(sem["m2nm"], 1)
                        e.wait_ge(sem["m2e"], hh * KC + KC)
                        for cb in range(4):
                            jw = hh * 4 + cb
                            e.wait_ge(sem["wpt0"], 256 * (jw + 1))
                            for tt in range(4):
                                j = hh * 16 + cb * 4 + tt
                                if j >= 4:
                                    e.wait_ge(sem["pre"], j - 3)
                                pp = ps[j % 4]
                                for fc in range(KC):
                                    e.matmul(pp[:, 0:512],
                                             attnF[:, fc * 512 + tt * 128: fc * 512 + (tt + 1) * 128],
                                             wpT[:, fc * 512:(fc + 1) * 512],
                                             start=(fc == 0), stop=False)
                                e.matmul(pp[:, 0:512], t_ones1[:, :],
                                         t_bpb[:, cb * 512:(cb + 1) * 512],
                                         start=False, stop=True).then_inc(sem["prm"], 1)

        if part == "attn":
            with nc.Block() as blk:
                @blk.sync
                def _(e):
                    for tc in range(NT):
                        e.wait_ge(sem["x2"], tc + 1)
                        e.dma_start(out[tc * 128:(tc + 1) * 128, :],
                                    x2[:, tc * C:(tc + 1) * C]).then_inc(sem["outs"], 16)
                    e.wait_ge(sem["outs"], 16 * NT)
            return nc

        x3F = octx.enter_context(nc.sbuf_tensor("x3F", [128, KC * TOK], BF16, side="right"))

        with ExitStack() as c4, nc.Block() as blk:
            x3ln = [c4.enter_context(nc.sbuf_tensor(f"x3ln{i}", [128, C], BF16)) for i in range(2)]
            lnt3 = c4.enter_context(nc.sbuf_tensor("lnt3", [128, 8], F32))
            scrF3 = c4.enter_context(nc.sbuf_tensor("scrF3", [128, C], F32))
            pbuf = [c4.enter_context(nc.sbuf_tensor(f"pbuf{i}", [128, 1024], I32)) for i in range(2)]
            sc_g = c4.enter_context(nc.sbuf_tensor("sc_g", [128, 1024], F32))
            sc_u = c4.enter_context(nc.sbuf_tensor("sc_u", [128, 1024], F32))
            sc_d = c4.enter_context(nc.sbuf_tensor("sc_d", [128, 1024], F32))
            lo = c4.enter_context(nc.sbuf_tensor("lo", [128, 1024], I32))
            hi = c4.enter_context(nc.sbuf_tensor("hi", [128, 1024], I32))
            cl = c4.enter_context(nc.sbuf_tensor("cl", [128, 1024], BF16))
            ch = c4.enter_context(nc.sbuf_tensor("ch", [128, 1024], BF16))
            dsg = c4.enter_context(nc.sbuf_tensor("dsg", [128, 2048], BF16))
            dw = c4.enter_context(nc.sbuf_tensor("dw", [128, 2048], BF16))
            dt1 = c4.enter_context(nc.sbuf_tensor("dt1", [128, 2048], BF16))
            dt2 = c4.enter_context(nc.sbuf_tensor("dt2", [128, 2048], BF16))
            dm = c4.enter_context(nc.sbuf_tensor("dm", [128, 2048], BF16))
            wout = [c4.enter_context(nc.sbuf_tensor(f"wout{i}", [128, 2048], BF16)) for i in range(2)]
            stg = [c4.enter_context(nc.sbuf_tensor(f"stg{i}", [128, 2048], BF16)) for i in range(2)]

            pieces = ([("g", i) for i in range(16)] + [("u", i) for i in range(16)]
                      + [("d", i) for i in range(16)])

            def stg_tot(b, p):
                t = 0
                for q2 in range(p + 1):
                    if q2 % 2 == b:
                        t += 32 if q2 >= 32 else 16
                return t
            PK = {"g": gwp, "u": uwp, "d": dwp}
            DQ = {"g": dq_g, "u": dq_u, "d": dq_d}
            SCT = {"g": sc_g, "u": sc_u, "d": sc_d}

            @blk.sync
            def _(e):
                e.dma_start(_r(sc_g[:], "p (q nb b) -> p q nb b", q=16, nb=8),
                            _r(gsc, "(q nb p) b -> p q nb b", p=128, nb=8)).then_inc(sem["sc"], 16)
                e.dma_start(_r(sc_u[:], "p (q nb b) -> p q nb b", q=16, nb=8),
                            _r(usc, "(q nb p) b -> p q nb b", p=128, nb=8)).then_inc(sem["sc"], 16)
                e.dma_start(_r(sc_d[:], "p (r b) -> p r b", r=16),
                            _r(dsc, "(r p) b -> p r b", p=128)).then_inc(sem["sc"], 16)
                for pi, (mat, q) in enumerate(pieces):
                    if pi >= 2:
                        e.wait_ge(sem["dql"], pi - 1)
                    if mat in ("g", "u"):
                        e.dma_start(_r(pbuf[pi % 2][:], "p (nb w) -> p nb w", w=128),
                                    _r(PK[mat][q * 1024:(q + 1) * 1024, :], "(nb p) w -> p nb w", p=128)
                                    ).then_inc(sem[f"pk{pi % 2}"], 16)
                    else:
                        e.dma_start(pbuf[pi % 2][:],
                                    PK[mat][q * 128:(q + 1) * 128, :]).then_inc(sem[f"pk{pi % 2}"], 16)

            def _evac_x3(e, tc):
                for kc in range(KC):
                    i = tc * KC + kc
                    e.wait_ge(sem["t3"], i + 1)
                    ins = e.tensor_copy(x3F[:, kc * TOK + tc * 128: kc * TOK + (tc + 1) * 128],
                                        psb[i % 2][:, 0:128])
                    if i == NT * KC - 1:
                        e.drain(fusable=False).then_inc(sem["t3e"], 1)
                    else:
                        ins.then_inc(sem["t3e"], 1)

            @blk.vector
            def _(e):
                for tc in range(NT):
                    e.wait_ge(sem["x2"], tc + 1)
                    xc = x2[:, tc * C:(tc + 1) * C]
                    _emit_ln_stats(e, lnt3, xc, scrF3[:]).then_inc(sem["ln3st"], 1)
                    e.wait_ge(sem["ln3rs"], tc + 1)
                    _emit_rstd(e, lnt3)
                    e.tensor_scalar(x3ln[tc % 2][:], xc, lnt3[:, 5:6], lnt3[:, 6:7], Alu.mult, Alu.add)
                    e.drain(fusable=False).then_inc(sem["ln3"], 1)
                    if tc >= 1:
                        _evac_x3(e, tc - 1)
                _evac_x3(e, NT - 1)
                e.wait_ge(sem["sc"], 48)
                for pi, (mat, q) in enumerate(pieces):
                    e.wait_ge(sem[f"pk{pi % 2}"], 16 * (pi // 2 + 1))
                    pb = pbuf[pi % 2]
                    e.tensor_scalar(lo[:], pb[:], 15, None, Alu.bitwise_and)
                    e.tensor_scalar(hi[:], pb[:], 4, 15, Alu.logical_shift_right, Alu.bitwise_and)
                    e.drain(fusable=False)
                    e.tensor_copy(cl[:], lo[:])
                    e.tensor_copy(ch[:], hi[:])
                    e.drain(fusable=False).then_inc(sem["dql"], 1)
                    e.wait_ge(sem["dqsg"], pi + 1)
                    e.scalar_tensor_tensor(dw[:, 0:1024], dsg[:, 0:1024], 4.0, cl[:], Alu.mult, Alu.add)
                    e.scalar_tensor_tensor(dw[:, 1024:2048], dsg[:, 1024:2048], 4.0, ch[:], Alu.mult, Alu.add)
                    e.drain(fusable=False).then_inc(sem["dqw"], 1)
                    e.wait_ge(sem["dqt"], pi + 1)
                    if pi >= 2:
                        e.wait_ge(sem["dqtr"], 16 * (pi - 1))
                    wo = wout[pi % 2]
                    e.scalar_tensor_tensor(dm[:], dw[:], 0.5, dt1[:], Alu.mult, Alu.add)
                    e.drain(fusable=False)
                    e.tensor_tensor(dm[:], dm[:], dt2[:], Alu.add)
                    e.drain(fusable=False)
                    e.scalar_tensor_tensor(dm[:], dm[:], -2.0, dsg[:], Alu.add, Alu.mult)
                    e.drain(fusable=False)
                    for half in range(2):
                        s = slice(half * 1024, (half + 1) * 1024)
                        wo_h = _r(wo[:], "p (x two) -> p x two", two=2)[:, :, half]
                        if mat in ("g", "u"):
                            sct = _r(_r(SCT[mat][:, q * 64:(q + 1) * 64], "p (nb b) -> p nb b", b=8),
                                     "p nb (b o) -> p nb b o", o=1).broadcast_to([128, 8, 8, 16])
                            dst = _r(wo_h, "p (nb b r) -> p nb b r", nb=8, b=8)
                            src = _r(dm[:, s], "p (nb b r) -> p nb b r", nb=8, b=8)
                        else:
                            sct = _r(SCT[mat][:, q * 64:(q + 1) * 64],
                                     "p (b o) -> p b o", o=1).broadcast_to([128, 64, 16])
                            dst = _r(wo_h, "p (b r) -> p b r", b=64)
                            src = _r(dm[:, s], "p (b r) -> p b r", b=64)
                        e.tensor_tensor(dst, src, sct, Alu.mult)
                    e.drain(fusable=False).then_inc(sem["dqv"], 1)
                    if pi >= 2:
                        e.wait_ge(sem[f"stg{pi % 2}"], stg_tot(pi % 2, pi - 2))
                    for j in range(16):
                        t = pi * 16 + j
                        e.wait_ge(sem["dqtr"], t + 1)
                        ins = e.tensor_copy(stg[pi % 2][:, j * 128:(j + 1) * 128], psb[t % 2][:, 0:128])
                        if j == 15:
                            e.drain(fusable=False).then_inc(sem["dqte"], 1)
                        else:
                            ins.then_inc(sem["dqte"], 1)

            @blk.scalar
            def _(e):
                for tc in range(NT):
                    e.wait_ge(sem["ln3st"], tc + 1)
                    e.activation(lnt3[:, 5:6], lnt3[:, 4:5], Act.Sqrt, bias=EPS_LN, scale=1.0)
                    e.drain(fusable=False).then_inc(sem["ln3rs"], 1)
                for pi in range(len(pieces)):
                    e.wait_ge(sem["dql"], pi + 1)
                    e.activation(dsg[:, 0:1024], cl[:], Act.Sign, bias=7.5, scale=-1.0)
                    e.activation(dsg[:, 1024:2048], ch[:], Act.Sign, bias=7.5, scale=-1.0).then_inc(sem["dqsg"], 1)
                    e.wait_ge(sem["dqw"], pi + 1)
                    e.activation(dt1[:], dw[:], Act.Relu, bias=-4.0, scale=0.5)
                    e.activation(dt2[:], dw[:], Act.Relu, bias=-10.0, scale=1.0).then_inc(sem["dqt"], 1)

            @blk.tensor
            def _(e):
                for tc in range(NT):
                    e.wait_ge(sem["ln3"], tc + 1)
                    for kc in range(KC):
                        i = tc * KC + kc
                        if i >= 2:
                            e.wait_ge(sem["t3e"], i - 1)
                        e.transpose(psb[i % 2][:, 0:128],
                                    x3ln[tc % 2][:, kc * 128:(kc + 1) * 128],
                                    t_ident[:]).then_inc(sem["t3"], 1)
                for pi in range(len(pieces)):
                    e.wait_ge(sem["dqv"], pi + 1)
                    for j in range(16):
                        t = pi * 16 + j
                        if t >= 2:
                            e.wait_ge(sem["dqte"], t - 1)
                        e.transpose(psb[t % 2][:, 0:128],
                                    wout[pi % 2][:, j * 128:(j + 1) * 128],
                                    t_ident[:]).then_inc(sem["dqtr"], 1)

            @blk.gpsimd
            def _(e):
                for pi, (mat, q) in enumerate(pieces):
                    e.wait_ge(sem["dqte"], 16 * (pi + 1))
                    if mat in ("g", "u"):
                        dst = _r(DQ[mat].ap(), "p (nb cbl n) -> p nb cbl n", nb=128, cbl=2)
                        e.dma_start(dst[:, 8 * q:8 * (q + 1), :, :],
                                    _r(stg[pi % 2][:], "p (nb cbl n) -> p nb cbl n", nb=8, cbl=2)
                                    ).then_inc(sem[f"stg{pi % 2}"], 16)
                    else:
                        dst = _r(DQ[mat].ap(), "p (nbl c) -> p nbl c", nbl=16)
                        src = _r(stg[pi % 2][:], "p (nbl c) -> p nbl c", nbl=16)
                        e.dma_start(dst[:, 0:8, q * 128:(q + 1) * 128],
                                    src[:, 0:8, :]).then_inc(sem[f"stg{pi % 2}"], 16)
                        e.dma_start(dst[:, 8:16, q * 128:(q + 1) * 128],
                                    src[:, 8:16, :]).then_inc(sem[f"stg{pi % 2}"], 16)
                    if pi == 15:
                        e.wait_ge(sem["stg0"], stg_tot(0, 15))
                        e.wait_ge(sem["stg1"], stg_tot(1, 15))
                        e.collective_compute("AllGather", Alu.bypass,
                                             replica_groups=[list(range(NCORES))],
                                             ins=[dq_g.ap().opt()],
                                             outs=[ag_g.ap().opt()]).then_inc(sem["cc"])
                    elif pi == 31:
                        e.wait_ge(sem["stg0"], stg_tot(0, 31))
                        e.wait_ge(sem["stg1"], stg_tot(1, 31))
                        e.collective_compute("AllGather", Alu.bypass,
                                             replica_groups=[list(range(NCORES))],
                                             ins=[dq_u.ap().opt()],
                                             outs=[ag_u.ap().opt()]).then_inc(sem["cc"])
                    elif pi == 47:
                        e.wait_ge(sem["stg0"], stg_tot(0, 47))
                        e.wait_ge(sem["stg1"], stg_tot(1, 47))
                        e.collective_compute("AllGather", Alu.bypass,
                                             replica_groups=[list(range(NCORES))],
                                             ins=[dq_d.ap().opt()],
                                             outs=[ag_d.ap().opt()]).then_inc(sem["cc"])

        with ExitStack() as c5, nc.Block() as blk:
            wg = [c5.enter_context(nc.sbuf_tensor(f"wg{i}", [128, KC * 128], BF16)) for i in range(2)]
            wu = [c5.enter_context(nc.sbuf_tensor(f"wu{i}", [128, KC * 128], BF16)) for i in range(2)]
            wd = c5.enter_context(nc.sbuf_tensor("wd", [128, 8 * C], BF16))
            inter = c5.enter_context(nc.sbuf_tensor("inter", [128, 8 * 512], BF16))
            gsb = [c5.enter_context(nc.sbuf_tensor(f"gsb{i}", [128, 512], BF16)) for i in range(2)]
            dacc = c5.enter_context(nc.sbuf_tensor("dacc", [128, 4 * C], F32))
            lnt4 = c5.enter_context(nc.sbuf_tensor("lnt4", [128, 8], F32))
            scrF4 = c5.enter_context(nc.sbuf_tensor("scrF4", [128, C], F32))

            ag_g5 = _r(ag_g.ap(), "(core p) (nb cbl n) -> p core nb cbl n", p=128, nb=128, cbl=2)
            ag_u5 = _r(ag_u.ap(), "(core p) (nb cbl n) -> p core nb cbl n", p=128, nb=128, cbl=2)

            @blk.sync
            def _(e):
                for th in range(2):
                    for grp in range(16):
                        gg = th * 16 + grp
                        for j in range(8):
                            nb = grp * 8 + j
                            ni = gg * 8 + j
                            if ni >= 2:
                                e.wait_ge(sem["fnb"], ni - 1)
                            if gg == 0 and j == 0:
                                e.wait_ge(sem["cc"], 2)
                            e.dma_start(_r(wg[ni % 2][:], "p (core cbl n) -> p core cbl n",
                                           core=NCORES, cbl=2),
                                        ag_g5[:, :, nb, :, :]).then_inc(sem[f"wg{ni % 2}"], 16)
                            if gg == 0 and j == 0:
                                e.wait_ge(sem["cc"], 3)
                            e.dma_start(_r(wu[ni % 2][:], "p (core cbl n) -> p core cbl n",
                                           core=NCORES, cbl=2),
                                        ag_u5[:, :, nb, :, :]).then_inc(sem[f"wu{ni % 2}"], 16)
                        if gg == 0:
                            e.wait_ge(sem["cc"], 4)
                        else:
                            e.wait_ge(sem["dnm"], 16 * gg)
                        for j in range(8):
                            nb = grp * 8 + j
                            core, nbl = nb // 16, nb % 16
                            e.dma_start(wd[:, j * C:(j + 1) * C],
                                        ag_d.ap()[core * 128:(core + 1) * 128,
                                                  nbl * C:(nbl + 1) * C]).then_inc(sem["wd"], 16)

            @blk.tensor
            def _(e):
                e.wait_ge(sem["pre"], 32)
                e.wait_ge(sem["t3e"], 128)
                for th in range(2):
                    for grp in range(16):
                        gg = th * 16 + grp
                        for j in range(8):
                            ni = gg * 8 + j
                            e.wait_ge(sem[f"wg{ni % 2}"], 16 * (ni // 2 + 1))
                            e.wait_ge(sem[f"wu{ni % 2}"], 16 * (ni // 2 + 1))
                            if ni >= 2:
                                e.wait_ge(sem["fna"], ni - 1)
                                e.wait_ge(sem["fne"], ni - 1)
                            for kc in range(KC):
                                ins = e.matmul(ps[ni % 2][:, 0:512],
                                               wg[ni % 2][:, kc * 128:(kc + 1) * 128],
                                               x3F[:, kc * TOK + th * 512: kc * TOK + th * 512 + 512],
                                               start=(kc == 0), stop=(kc == KC - 1))
                            for kc in range(KC):
                                ins = e.matmul(ps[2 + ni % 2][:, 0:512],
                                               wu[ni % 2][:, kc * 128:(kc + 1) * 128],
                                               x3F[:, kc * TOK + th * 512: kc * TOK + th * 512 + 512],
                                               start=(kc == 0), stop=(kc == KC - 1))
                            ins.then_inc(sem["fnb"], 1)
                        e.wait_ge(sem["wd"], 128 * (gg + 1))
                        e.wait_ge(sem["fne"], gg * 8 + 8)
                        for tq in range(4):
                            for cb in range(4):
                                dj = gg * 16 + tq * 4 + cb
                                if dj >= 2:
                                    e.wait_ge(sem["dne"], dj - 1)
                                for j in range(8):
                                    ins = e.matmul(ps[4 + dj % 2][:, 0:512],
                                                   inter[:, j * 512 + tq * 128: j * 512 + (tq + 1) * 128],
                                                   wd[:, j * C + cb * 512: j * C + (cb + 1) * 512],
                                                   start=(j == 0), stop=(j == 7))
                                ins.then_inc(sem["dnm"], 1)

            @blk.scalar
            def _(e):
                for th in range(2):
                    for grp in range(16):
                        gg = th * 16 + grp
                        for j in range(8):
                            ni = gg * 8 + j
                            e.wait_ge(sem["fnb"], ni + 1)
                            e.activation(gsb[ni % 2][:], ps[ni % 2][:, 0:512], Act.Gelu_apprx_tanh)
                            e.drain(fusable=False).then_inc(sem["fna"], 1)
                    for tq in range(4):
                        e.wait_ge(sem["ln4st"], th * 4 + tq + 1)
                        e.activation(lnt4[:, 5:6], lnt4[:, 4:5], Act.Sqrt, bias=EPS_LN, scale=1.0)
                        e.drain(fusable=False).then_inc(sem["ln4rs"], 1)

            @blk.vector
            def _(e):
                for th in range(2):
                    e.memset(dacc[:], 0.0)
                    e.drain(fusable=False)
                    for grp in range(16):
                        gg = th * 16 + grp
                        for j in range(8):
                            ni = gg * 8 + j
                            e.wait_ge(sem["fna"], ni + 1)
                            ins = e.tensor_tensor(inter[:, j * 512:(j + 1) * 512], gsb[ni % 2][:],
                                                  ps[2 + ni % 2][:, 0:512], Alu.mult)
                            if j == 7:
                                e.drain(fusable=False).then_inc(sem["fne"], 1)
                            else:
                                ins.then_inc(sem["fne"], 1)
                        for tq in range(4):
                            for cb in range(4):
                                dj = gg * 16 + tq * 4 + cb
                                e.wait_ge(sem["dnm"], dj + 1)
                                dslc = dacc[:, tq * C + cb * 512: tq * C + (cb + 1) * 512]
                                e.scalar_tensor_tensor(dslc, ps[4 + dj % 2][:, 0:512], 1.0,
                                                       dslc, Alu.mult, Alu.add).then_inc(sem["dne"], 1)
                    e.drain(fusable=False)
                    for tq in range(4):
                        tc = th * 4 + tq
                        dc = dacc[:, tq * C:(tq + 1) * C]
                        _emit_ln_stats(e, lnt4, dc, scrF4[:]).then_inc(sem["ln4st"], 1)
                        e.wait_ge(sem["ln4rs"], tc + 1)
                        _emit_rstd(e, lnt4)
                        xc = x2[:, tc * C:(tc + 1) * C]
                        e.tensor_scalar(xc, xc, lnt4[:, 6:7], None, Alu.add)
                        e.drain(fusable=False)
                        e.scalar_tensor_tensor(xc, dc, lnt4[:, 5:6], xc, Alu.mult, Alu.add)
                        e.drain(fusable=False).then_inc(sem["x2"], 1)

            @blk.sync
            def _(e):
                for tc in range(NT):
                    e.wait_ge(sem["x2"], NT + tc + 1)
                    e.dma_start(out[tc * 128:(tc + 1) * 128, :],
                                x2[:, tc * C:(tc + 1) * C]).then_inc(sem["outs"], 16)
                e.wait_ge(sem["outs"], 16 * NT)

    return nc


def _make_consts():
    ident = np.eye(128, dtype=ml_dtypes.bfloat16)
    sel64 = np.zeros((64, KC * 128), dtype=ml_dtypes.bfloat16)
    for cc in range(KC):
        for m in range(128):
            sel64[4 * cc + m // 32, cc * 128 + m] = 1.0
    cf32 = np.tile(np.array([CONSTS], dtype=np.float32), (128, 1))
    cbf = cf32.astype(ml_dtypes.bfloat16)
    return ident, sel64, cf32, cbf


def _in_maps(inputs):
    x = np.asarray(inputs["L0_attn_x"])
    wqkv_f = np.asarray(inputs["L0_attn_w_qkv"])
    wproj_f = np.asarray(inputs["L0_attn_w_proj"])
    bproj_f = np.asarray(inputs["L0_attn_b_proj"]).reshape(1, C)
    gwp_f = np.asarray(inputs["L0_ffn_gate_W_packed"])
    gsc_f = np.asarray(inputs["L0_ffn_gate_scales"])
    uwp_f = np.asarray(inputs["L0_ffn_up_W_packed"])
    usc_f = np.asarray(inputs["L0_ffn_up_scales"])
    dwp_f = np.asarray(inputs["L0_ffn_down_W_packed"])
    dsc_f = np.asarray(inputs["L0_ffn_down_scales"])
    ident, sel64, cf32, cbf = _make_consts()
    maps = []
    for i in range(NCORES):
        maps.append(dict(
            x_sl=np.ascontiguousarray(x[0, i * TOK:(i + 1) * TOK, :]),
            wqkv=wqkv_f, wproj=wproj_f, bproj=bproj_f,
            gwp=np.ascontiguousarray(gwp_f[:, i * 128:(i + 1) * 128]),
            gsc=np.ascontiguousarray(gsc_f[:, i * 8:(i + 1) * 8]),
            uwp=np.ascontiguousarray(uwp_f[:, i * 128:(i + 1) * 128]),
            usc=np.ascontiguousarray(usc_f[:, i * 8:(i + 1) * 8]),
            dwp=np.ascontiguousarray(dwp_f[:, i * 1024:(i + 1) * 1024]),
            dsc=np.ascontiguousarray(dsc_f[:, i * 64:(i + 1) * 64]),
            cident=ident, csel64=sel64, ccf32=cf32, ccbf=cbf,
        ))
    return maps


def _run(inputs, part="full"):
    nc = build_program(part=part)
    maps = _in_maps(inputs)
    res = run_bass_kernel_spmd(nc, maps, core_ids=list(range(NCORES)))
    return np.concatenate([r["out"] for r in res.results], axis=0).reshape(1, NCORES * TOK, C)


def kernel(**inputs):
    return _run(inputs, part="full")
